# version 01 (final)
# baseline (speedup 1.0000x reference)
import numpy as np
from contextlib import ExitStack

import concourse.bass as bass
import concourse.mybir as mybir
from concourse.bass_utils import run_bass_kernel_spmd

F32 = mybir.dt.float32
BF16 = mybir.dt.bfloat16
FP16 = mybir.dt.float16
AF = mybir.ActivationFunctionType
ALU = mybir.AluOpType
AX = mybir.AxisListType

D = 1024
T = 2048
DH = 4096
NCH = D // 128
LN2046 = float(np.log(2046.0))
EPS = 1e-5

ENGINES = ("te", "ve", "sc", "gp", "sy")
QUEUES = ("gq", "yq")
_ISSUER = {"te": "tensor", "ve": "vector", "sc": "scalar", "gp": "gpsimd",
           "sy": "sync", "gq": "gpsimd", "yq": "sync"}


class _Op:
    def __init__(self, stream, fn, deps, amount, gidx):
        self.stream = stream
        self.fn = fn
        self.deps = deps
        self.amount = amount
        self.gidx = gidx
        self.awaited = False
        self.cum = 0


class Prog:
    def __init__(self, nc):
        self.nc = nc
        self.ops = {s: [] for s in ENGINES + QUEUES}
        self._g = 0

    def _rec(self, stream, fn, deps, amount):
        deps = [d for d in deps if d is not None]
        op = _Op(stream, fn, deps, amount, self._g)
        self._g += 1
        self.ops[stream].append(op)
        return op

    def te(self, fn, deps=()):
        return self._rec("te", fn, deps, 1)

    def ve(self, fn, deps=()):
        return self._rec("ve", fn, deps, 1)

    def sc(self, fn, deps=()):
        return self._rec("sc", fn, deps, 1)

    def gp(self, fn, deps=()):
        return self._rec("gp", fn, deps, 1)

    def gdma(self, out, in_, deps=()):
        return self._rec("gq", lambda e: e.dma_start(out=out, in_=in_), deps, 16)

    def ydma(self, out, in_, deps=()):
        return self._rec("yq", lambda e: e.dma_start(out=out, in_=in_), deps, 16)

    def barrier(self, streams=ENGINES):
        lasts = []
        for s in ENGINES:
            if self.ops[s]:
                lasts.append(self.ops[s][-1])
        for s in QUEUES:
            lasts.extend(self.ops[s][-self.QPOOL:])
        return [self._rec(s, lambda e: e.nop(), list(lasts), 1) for s in streams]

    QPOOL = 10

    def emit(self, block, ctx):
        for ops in self.ops.values():
            for op in ops:
                for d in op.deps:
                    if d.stream != op.stream:
                        d.awaited = True
        sems = {}
        for s in ENGINES:
            if not self.ops[s]:
                continue
            sems[s] = ctx.enter_context(self.nc.semaphore(f"sem_{s}"))
            c = 0
            for op in self.ops[s]:
                if op.awaited:
                    c += op.amount
                op.cum = c
        qsems = {}
        for s in QUEUES:
            if not self.ops[s]:
                continue
            qsems[s] = [ctx.enter_context(self.nc.semaphore(f"sem_{s}{i}"))
                        for i in range(self.QPOOL)]
            for op in self.ops[s]:
                op.qidx = op.seq_in_q = None
            for i, op in enumerate(self.ops[s]):
                op.qidx = i % self.QPOOL
                op.qgen = i // self.QPOOL

        body_streams = {}
        for s in ENGINES + QUEUES:
            if self.ops[s]:
                body_streams.setdefault(_ISSUER[s], []).append(s)

        def make_body(streams):
            merged = []
            for s in streams:
                merged.extend(self.ops[s])
            merged.sort(key=lambda o: o.gidx)

            def body(eng):
                seen = {}
                for op in merged:
                    waits = {}
                    for d in op.deps:
                        if d.stream == op.stream:
                            continue
                        if d.stream in QUEUES:
                            key = (d.stream, d.qidx)
                            waits[key] = max(waits.get(key, 0), 16 * (d.qgen + 1))
                        else:
                            waits[d.stream] = max(waits.get(d.stream, 0), d.cum)
                    if op.stream in QUEUES and op.qgen >= 1:
                        key = (op.stream, op.qidx)
                        waits[key] = max(waits.get(key, 0), 16 * op.qgen)
                    for s2, v in sorted(waits.items(), key=str):
                        if seen.get(s2, -1) >= v:
                            continue
                        sem = (qsems[s2[0]][s2[1]] if isinstance(s2, tuple)
                               else sems[s2])
                        eng.wait_ge(sem, v)
                        seen[s2] = v
                    ins = op.fn(eng)
                    if op.stream in QUEUES:
                        ins.then_inc(qsems[op.stream][op.qidx], 16)
                    elif op.awaited:
                        ins.then_inc(sems[op.stream], op.amount)
            return body

        for body_name, streams in sorted(body_streams.items()):
            getattr(block, body_name)(make_body(streams))

    def final_waits(self):
        return [ops[-1] for ops in self.ops.values() if ops]


class B:

    def __init__(self, nc, ctx):
        self.nc = nc
        self.P = Prog(nc)
        self.ctx = ctx
        self.c = {}

    def sb(self, name, shape, dtype, stack=None, side=None):
        return (stack or self.ctx).enter_context(
            self.nc.sbuf_tensor(name, shape, dtype, side=side))

    def ps(self, name, shape, dtype=F32, stack=None, side=None):
        return (stack or self.ctx).enter_context(
            self.nc.psum_tensor(name, shape, dtype, side=side))


def emit_prologue(b: B):
    P = b.P
    b.c["ones_col_bf"] = b.sb("c_ones_col_bf", [128, 1], BF16)
    b.c["ones_row_f32"] = b.sb("c_ones_row_f32", [1, 128], F32)
    b.c["ones_row64"] = b.sb("c_ones_row64", [1, 64], F32)
    b.c["eps"] = b.sb("c_eps", [1, 1], F32)
    b.c["pool"] = b.sb("c_pool", [128, 8], F32)
    b.c["ident_f32"] = b.sb("c_ident", [128, 128], F32)
    b.c["ident_bf"] = b.sb("c_ident_bf", [128, 128], BF16)
    p = []
    b.c["ones_colf"] = b.sb("c_ones_colf", [128, 1], F32)
    p.append(P.ve(lambda e: e.memset(b.c["ones_col_bf"][:], 1.0)))
    p.append(P.ve(lambda e: e.memset(b.c["ones_colf"][:], 1.0)))
    p.append(P.ve(lambda e: e.memset(b.c["ones_row_f32"][:], 1.0)))
    p.append(P.ve(lambda e: e.memset(b.c["ones_row64"][:], 1.0)))
    p.append(P.ve(lambda e: e.memset(b.c["eps"][:], EPS)))
    b.c["_identcp"] = None
    b.c["_ready"] = p


def emit_layernorm(b: B, x_hbm, t0, W, xn, tag, out_rb=None):
    P, c = b.P, b.c
    st = ExitStack()
    xtmp = [b.sb(f"{tag}xt{i}", [128, W], F32, st, side="right") for i in range(2)]
    xb = [b.sb(f"{tag}xb{i}", [128, W], BF16, st, side="right") for i in range(2)]
    sq = [b.sb(f"{tag}sq{i}", [128, W], BF16, st, side="right") for i in range(2)]
    rows = b.sb(f"{tag}rows", [1, 4 * W], F32, st, side="right")
    m_row = rows[:, 0:W]
    v_row = rows[:, W:2 * W]
    r_row = rows[:, 2 * W:3 * W]
    b_row = rows[:, 3 * W:4 * W]
    Rcol = b.sb(f"{tag}Rcol", [128, W], F32, st, side="right")
    Bcol = b.sb(f"{tag}Bcol", [128, W], F32, st, side="right")
    ps_mean = b.ps(f"{tag}psm", [1, W], F32, st)
    ps_var = b.ps(f"{tag}psv", [1, W], F32, st)
    ps_r = b.ps(f"{tag}psr", [128, W], F32, st)
    nnt = W // 512

    lds, casts, sqs, mm_done = [], [], [], []
    mm_last = None
    for cc in range(NCH):
        buf = cc % 2
        guard = [casts[cc - 2], sqs[cc - 2]] if cc >= 2 else []
        ld = P.ydma(xtmp[buf][:], x_hbm[cc, :, t0:t0 + W], deps=guard)
        lds.append(ld)
        cast = P.ve(lambda e, bu=buf: e.tensor_copy(xb[bu][:], xtmp[bu][:]),
                    deps=[ld] + ([mm_done[cc - 2]] if cc >= 2 else c["_ready"]))
        casts.append(cast)
        sqo = P.sc(lambda e, bu=buf: e.activation(sq[bu][:], xtmp[bu][:], AF.Square),
                   deps=[ld] + ([mm_done[cc - 2]] if cc >= 2 else []))
        sqs.append(sqo)
        for nt in range(nnt):
            P.te(lambda e, bu=buf, n=nt, s=(cc == 0), p=(cc == NCH - 1):
                 e.matmul(ps_mean[:, 512 * n:512 * (n + 1)], c["ones_col_bf"][:],
                          xb[bu][:, 512 * n:512 * (n + 1)], start=s, stop=p),
                 deps=[cast])
            mm_last = P.te(lambda e, bu=buf, n=nt, s=(cc == 0), p=(cc == NCH - 1):
                           e.matmul(ps_var[:, 512 * n:512 * (n + 1)], c["ones_col_bf"][:],
                                    sq[bu][:, 512 * n:512 * (n + 1)], start=s, stop=p),
                           deps=[sqo])
        mm_done.append(mm_last)

    v1 = P.ve(lambda e: e.tensor_scalar_mul(m_row, ps_mean[:], 1.0 / D), deps=[mm_last])
    v2 = P.ve(lambda e: e.tensor_scalar_mul(v_row, ps_var[:], 1.0 / D), deps=[mm_last])
    v3 = P.ve(lambda e: e.scalar_tensor_tensor(r_row, m_row, -1.0, m_row, ALU.mult, ALU.mult))
    v4 = P.ve(lambda e: e.tensor_add(v_row, v_row, r_row))
    s1 = P.sc(lambda e: e.activation(r_row, v_row, AF.Ln, bias=c["eps"][:]), deps=[v4])
    s2 = P.sc(lambda e: e.activation(r_row, r_row, AF.Exp, scale=-0.5))
    v5 = P.ve(lambda e: e.scalar_tensor_tensor(b_row, m_row, -1.0, r_row, ALU.mult, ALU.mult),
              deps=[s2])
    if out_rb is not None:
        P.ve(lambda e: e.tensor_copy(out_rb[0], r_row), deps=[s2])
        P.ve(lambda e: e.tensor_copy(out_rb[1], b_row), deps=[v5])
    tR = tB = None
    for nt in range(nnt):
        tR = P.te(lambda e, n=nt: e.matmul(ps_r[:, 512 * n:512 * (n + 1)], c["ones_row_f32"][:],
                                           r_row[:, 512 * n:512 * (n + 1)], start=True, stop=True),
                  deps=[s2])
    eR = P.ve(lambda e: e.tensor_copy(Rcol[:], ps_r[:]), deps=[tR])
    for nt in range(nnt):
        tB = P.te(lambda e, n=nt: e.matmul(ps_r[:, 512 * n:512 * (n + 1)], c["ones_row_f32"][:],
                                           b_row[:, 512 * n:512 * (n + 1)], start=True, stop=True),
                  deps=[v5, eR])
    eB = P.ve(lambda e: e.tensor_copy(Bcol[:], ps_r[:]), deps=[tB])

    outs = []
    prev_pair = {}
    for cc in range(NCH):
        buf = cc % 2
        guard = [prev_pair[cc - 2]] if cc >= 2 else [casts[-1], sqs[-1], mm_last]
        ld = P.ydma(xtmp[buf][:], x_hbm[cc, :, t0:t0 + W], deps=guard)
        mul = P.ve(lambda e, bu=buf: e.tensor_mul(xtmp[bu][:], xtmp[bu][:], Rcol[:]),
                   deps=[ld, eR])
        add = P.ve(lambda e, bu=buf, ch=cc: e.tensor_add(xn[:, ch, :], xtmp[bu][:], Bcol[:]),
                   deps=[eB])
        prev_pair[cc] = add
        outs.append(add)
    P.barrier()
    st.close()
    return outs


def emit_ffn_tile(b: B, x_src, x_dst, t0, W, wts, tag):
    P, c = b.P, b.c
    st = ExitStack()
    xn = b.sb(f"{tag}xn", [128, NCH, W], BF16, st)
    ln_done = emit_layernorm(b, x_src, t0, W, xn, tag + "ln")

    nnt = W // 512
    hA = b.sb(f"{tag}hA", [128, 32 * W], BF16, st)
    hB = b.sb(f"{tag}hB", [128, 32 * W], BF16, st)
    wbuf = [b.sb(f"{tag}wb{i}", [128, 4096], BF16, st) for i in range(2)]
    ps = [b.ps(f"{tag}ps{i}", [128, 512], F32, st) for i in range(4)]

    def matblock(src_dram, n_oc, n_kc, rhs_tile, evac, deps_rhs):
        evs = {}
        mms = {}
        for oc in range(n_oc):
            wb = oc % 2
            guard = [mms[oc - 2]] if oc >= 2 else []
            ld = P.gdma(wbuf[wb][:, 0:n_kc * 128], src_dram[oc], deps=guard)
            last = None
            for nt in range(nnt):
                pt = ps[2 * wb + nt]
                pguard = [evs[(oc - 2, nt)]] if oc >= 2 else []
                for kc in range(n_kc):
                    last = P.te(lambda e, w=wb, k=kc, n=nt, p=pt,
                                s=(kc == 0), q=(kc == n_kc - 1):
                                e.matmul(p[:], wbuf[w][:, 128 * k:128 * (k + 1)],
                                         rhs_tile(k)[:, 512 * n:512 * (n + 1)],
                                         start=s, stop=q),
                                deps=([ld] + pguard + (list(deps_rhs) if oc == 0 else []))
                                if kc == 0 else [])
                evs[(oc, nt)] = evac(oc, nt, pt, last)
            mms[oc] = last
        return evs

    def ev_w1(oc, nt, pt, mm):
        sl = slice(W * oc + 512 * nt, W * oc + 512 * (nt + 1))
        return P.sc(lambda e: e.activation(hB[:, sl], pt[:], AF.Silu,
                                           scale=wts["w1_s"][:, oc:oc + 1]),
                    deps=[mm])
    ev1 = matblock(wts["w1"], 32, NCH, lambda k: xn[:, k, :], ev_w1, ln_done)
    P.barrier()

    def ev_w2(oc, nt, pt, mm):
        sl = slice(W * oc + 512 * nt, W * oc + 512 * (nt + 1))
        return P.ve(lambda e: e.scalar_tensor_tensor(hA[:, sl], pt[:], wts["w2_s"][:, oc:oc + 1],
                                                     hB[:, sl], ALU.mult, ALU.mult),
                    deps=[mm, ev1[(oc, nt)]])
    ev2 = matblock(wts["w2"], 32, NCH, lambda k: xn[:, k, :], ev_w2, [])
    P.barrier()

    def ev_h1(oc, nt, pt, mm):
        sl = slice(W * oc + 512 * nt, W * oc + 512 * (nt + 1))
        return P.sc(lambda e: e.activation(hB[:, sl], pt[:], AF.Gelu,
                                           scale=wts["h1_s"][:, oc:oc + 1]),
                    deps=[mm])
    ev3 = matblock(wts["h1"], 32, 32, lambda k: hA[:, W * k:W * (k + 1)], ev_h1, [])
    P.barrier()

    def ev_h2(oc, nt, pt, mm):
        sl = slice(W * oc + 512 * nt, W * oc + 512 * (nt + 1))
        return P.sc(lambda e: e.activation(hA[:, sl], pt[:], AF.Gelu,
                                           scale=wts["h2_s"][:, oc:oc + 1]),
                    deps=[mm])
    ev4 = matblock(wts["h2"], 32, 32, lambda k: hB[:, W * k:W * (k + 1)], ev_h2, [])
    P.barrier()

    xr = [b.sb(f"{tag}xr{i}", [128, W], F32, st) for i in range(2)]
    nx = [b.sb(f"{tag}nx{i}", [128, W], F32, st) for i in range(2)]
    xlds = {}
    stores = []

    def ev_w3(oc, nt, pt, mm):
        if oc not in xlds:
            guard = [stores[2 * (oc - 2) + 1]] if oc >= 2 else []
            xlds[oc] = P.ydma(xr[oc % 2][:], x_src[oc, :, t0:t0 + W], deps=guard)
        v = P.ve(lambda e: e.scalar_tensor_tensor(
            nx[oc % 2][:, 512 * nt:512 * (nt + 1)], pt[:], wts["w3_s"][:, oc:oc + 1],
            xr[oc % 2][:, 512 * nt:512 * (nt + 1)], ALU.mult, ALU.add),
            deps=[mm, xlds[oc]] + ([stores[2 * (oc - 2) + nt]] if oc >= 2 else []))
        stores.append(P.ydma(x_dst[oc, :, t0 + 512 * nt: t0 + 512 * (nt + 1)],
                             nx[oc % 2][:, 512 * nt:512 * (nt + 1)], deps=[v]))
        return v
    matblock(wts["w3"], NCH, 32, lambda k: hA[:, W * k:W * (k + 1)], ev_w3, [])
    st.close()
    return stores


def emit_attn_a(b: B, x_src, x_dst, dr, tag):
    P, c = b.P, b.c
    st = ExitStack()
    gros = ExitStack()
    xst = ExitStack()
    grows = b.sb(f"{tag}grows", [1, 2 * T], F32, gros, side="right")
    r_full = grows[:, 0:T]
    b_full = grows[:, T:2 * T]
    xn = b.sb(f"{tag}xn", [128, NCH, T], BF16, xst, side="right")
    for t0 in (0, T // 2):
        W2_ = T // 2
        emit_layernorm(b, x_src, t0, W2_, xn[:, :, t0:t0 + W2_], tag + f"l{t0 // 1024}",
                       out_rb=(r_full[:, t0:t0 + W2_], b_full[:, t0:t0 + W2_]))
    P.barrier()

    Q = b.sb(f"{tag}Q", [128, NCH, T], BF16, st)
    K = b.sb(f"{tag}K", [128, NCH, T], BF16, st)
    Vaug = b.sb(f"{tag}V", [128, 16, 1040], BF16, st)
    Wgf = b.sb(f"{tag}Wgf", [128, NCH], F32, st)
    bgc = b.sb(f"{tag}bgc", [128, 1], F32, st)

    pst = ExitStack()
    Wq = b.sb(f"{tag}Wq", [128, NCH, D], BF16, pst, side="right")
    Wk = b.sb(f"{tag}Wk", [128, NCH, D], BF16, pst, side="right")
    Wv = b.sb(f"{tag}Wv", [128, NCH, D], BF16, pst, side="right")
    bq8 = b.sb(f"{tag}bq8", [128, NCH], F32, pst, side="right")
    bkc = b.sb(f"{tag}bkc", [128, NCH], F32, pst, side="right")
    bvr = b.sb(f"{tag}bvr", [1, D], F32, pst, side="right")
    BVc = b.sb(f"{tag}BVc", [128, D], F32, pst, side="right")
    pp = [b.ps(f"{tag}pp{i}", [128, 512], F32, pst) for i in range(4)]
    lwq = P.gdma(Wq[:], dr["aWq"])
    lwk = P.gdma(Wk[:], dr["aWk"])
    lwv = P.gdma(Wv[:], dr["aWv"])
    lwg = P.ydma(Wgf[:], dr["aWg"])
    lbq = P.ydma(bq8[:], dr["abq"])
    lbk = P.ydma(bkc[:], dr["abk"])
    lbg = P.ydma(bgc[:], dr["abg"])
    lbv = P.ydma(bvr[:], dr["abv"])
    sc8 = P.ve(lambda e: e.tensor_scalar_mul(bq8[:], bq8[:], 0.125), deps=[lbq])
    vinit = P.ve(lambda e: e.memset(Vaug[:], 1.0))
    pbv = []
    for nt in range(2):
        mmv = P.te(lambda e, n=nt: e.matmul(pp[2 + n][:], c["ones_row_f32"][:],
                                            bvr[:, 512 * n:512 * (n + 1)],
                                            start=True, stop=True), deps=[lbv])
        pbv.append(P.ve(lambda e, n=nt: e.tensor_copy(BVc[:, 512 * n:512 * (n + 1)],
                                                      pp[2 + n][:]), deps=[mmv]))
    evs = {}
    pp_user = {}
    for (name, W_, lw_, bias, scale, outT) in (("q", Wq, lwq, bq8, 0.125, Q),
                                               ("k", Wk, lwk, bkc, None, K)):
        for oc in range(NCH):
            for nt in range(4):
                pt = pp[nt % 2]
                last = None
                for kc in range(NCH):
                    last = P.te(lambda e, W2=W_, o=oc, n=nt, k=kc, p=pt,
                                s=(kc == 0), q=(kc == NCH - 1):
                                e.matmul(p[:], W2[:, k, 128 * o:128 * (o + 1)],
                                         xn[:, k, 512 * n:512 * (n + 1)], start=s, stop=q),
                                deps=[lw_, sc8, pp_user.get(nt % 2)] if kc == 0 else [])
                evs[(name, oc, nt)] = pp_user[nt % 2] = P.ve(
                    lambda e, o=oc, n=nt, p=pt, bi=bias, T_=outT, s_=scale:
                    (e.tensor_scalar(T_[:, o, 512 * n:512 * (n + 1)], p[:],
                                     s_, bi[:, o:o + 1], ALU.mult, ALU.add) if s_ else
                     e.tensor_scalar_add(T_[:, o, 512 * n:512 * (n + 1)], p[:],
                                         bi[:, o:o + 1])),
                    deps=[last])
    for tc in range(16):
        for hf in range(2):
            pt = pp[2 + hf]
            last = None
            for kc in range(NCH):
                last = P.te(lambda e, t=tc, n=hf, k=kc, p=pt,
                            s=(kc == 0), q=(kc == NCH - 1):
                            e.matmul(p[:], xn[:, k, 128 * t:128 * (t + 1)],
                                     Wv[:, k, 512 * n:512 * (n + 1)], start=s, stop=q),
                            deps=[lwv, vinit, pp_user.get(2 + hf),
                                  pbv[0], pbv[1]] if kc == 0 else [])
            lastv = None
            for hh in range(8):
                h = 8 * hf + hh
                lastv = P.ve(lambda e, t=tc, hh2=hh, h2=h, p=pt:
                             e.tensor_add(Vaug[:, t, 65 * h2:65 * h2 + 64],
                                          p[:, 64 * hh2:64 * (hh2 + 1)],
                                          BVc[:, 64 * h2:64 * h2 + 64]),
                             deps=[last])
            evs[("v", tc, hf)] = pp_user[2 + hf] = lastv
    P.barrier()
    pst.close()
    if "dbg_xn" in dr:
        P.ydma(dr["dbg_xn"], xn[:])
        P.ydma(dr["dbg_Q"], Q[:])
        P.ydma(dr["dbg_K"], K[:])
        P.ydma(dr["dbg_V"], Vaug[:])
        P.barrier()

    Acol = b.sb(f"{tag}Acol", [128, T], FP16, st)
    Gcol = b.sb(f"{tag}Gcol", [128, T], FP16, st)
    PBcol = b.sb(f"{tag}PBcol", [128, T], F32, st)
    gT = b.sb(f"{tag}gT", [128, 16], F32, st)

    gst = ExitStack()
    rows = b.sb(f"{tag}rows", [1, 4 * T], F32, gst, side="right")
    raw_row = rows[:, 0:T]
    g_row = rows[:, T:2 * T]
    a_row = rows[:, 2 * T:3 * T]
    pb2_row = rows[:, 3 * T:4 * T]
    swg = b.sb(f"{tag}swg", [1, NCH + 1], F32, gst, side="right")
    t8 = b.sb(f"{tag}t8", [8, 1], F32, gst, side="right")
    xg = [b.sb(f"{tag}xg{i}", [128, 512], F32, gst, side="right") for i in range(2)]
    _xgi = [0]
    ps_g = b.ps(f"{tag}psg", [1, T], F32, gst)
    ps_sw = b.ps(f"{tag}pssw", [1, NCH], F32, gst)
    ps_sw8 = b.ps(f"{tag}pssw8", [8, 1], F32, gst)
    ps_b1 = b.ps(f"{tag}psb1", [128, 512], F32, gst)
    ps_gt = b.ps(f"{tag}psgt", [128, 1], F32, gst)

    sw_mm = P.te(lambda e: e.matmul(ps_sw8[:], Wgf[:], c["ones_colf"][:],
                                    start=True, stop=True), deps=[lwg])
    sw_e = P.ve(lambda e: e.tensor_copy(t8[:], ps_sw8[:]), deps=[sw_mm])
    sw_mm2 = P.te(lambda e: e.matmul(ps_sw[0:1, 0:1], c["ones_colf"][0:8, :], t8[:],
                                     start=True, stop=True), deps=[sw_e])
    sw_r = P.ve(lambda e: e.tensor_copy(swg[:, NCH:NCH + 1], ps_sw[0:1, 0:1]),
                deps=[sw_mm2])
    glast = None
    prev = {}
    for nt in range(4):
        for kc in range(NCH):
            i2 = _xgi[0]; _xgi[0] += 1
            buf = i2 % 2
            ldk = P.ydma(xg[buf][:], x_src[kc, :, 512 * nt:512 * (nt + 1)],
                         deps=[prev.get(i2 - 2)])
            glast = P.te(lambda e, n=nt, k=kc, bu=buf, s=(kc == 0), q=(kc == NCH - 1):
                         e.matmul(ps_g[:, 512 * n:512 * (n + 1)], Wgf[:, k:k + 1],
                                  xg[bu][:], start=s, stop=q),
                         deps=[ldk, lwg])
            prev[i2] = glast
    ge = P.ve(lambda e: e.tensor_copy(raw_row, ps_g[:]), deps=[glast])
    if "dbg_raw" in dr:
        dmp1 = P.ydma(dr["dbg_raw"], raw_row, deps=[ge])
        dmp2 = P.ydma(dr["dbg_rb"], grows[:], deps=[ge])
        dmp3 = P.ydma(dr["dbg_swg"], swg[:], deps=[sw_r])
        ge = dmp1
    gm = P.ve(lambda e: e.tensor_mul(raw_row, raw_row, r_full), deps=[ge])
    gm2 = P.ve(lambda e: e.scalar_tensor_tensor(g_row, b_full, swg[:, NCH:NCH + 1],
                                                raw_row, ALU.mult, ALU.add), deps=[sw_r])
    sg = P.sc(lambda e: e.activation(g_row, g_row, AF.Sigmoid, bias=bgc[0:1, 0:1]),
              deps=[gm2, lbg])
    gtprev = sg
    for j in range(16):
        ttr = P.te(lambda e, j2=j: e.transpose(ps_gt[:], g_row[:, 128 * j2:128 * (j2 + 1)],
                                               c["ident_f32"][0:1, 0:1]), deps=[gtprev, sg])
        gtprev = P.ve(lambda e, j2=j: e.tensor_copy(gT[:, j2:j2 + 1], ps_gt[:]), deps=[ttr])
    va = P.ve(lambda e: e.tensor_scalar(a_row, g_row, -1.0, 1.0, ALU.mult, ALU.add),
              deps=[sg])
    sp1 = P.sc(lambda e: e.activation(pb2_row, a_row, AF.Exp, scale=LN2046), deps=[va])
    vp2 = P.ve(lambda e: e.tensor_scalar_add(pb2_row, pb2_row, 2.0), deps=[sp1])
    bch = [sg]
    for (row, col, dep) in ((a_row, Acol, va), (g_row, Gcol, sg), (pb2_row, PBcol, vp2)):
        for nt in range(4):
            mm = P.te(lambda e, r=row, n=nt: e.matmul(
                ps_b1[:], c["ones_row_f32"][:], r[:, 512 * n:512 * (n + 1)],
                start=True, stop=True), deps=[dep, bch[-1]])
            bch.append(P.ve(lambda e, co=col, n=nt: e.tensor_copy(
                co[:, 512 * n:512 * (n + 1)], ps_b1[:]), deps=[mm]))
    if "dbg_xn" in dr:
        P.ydma(dr["dbg_g"], g_row, deps=[sg])
        P.ydma(dr["dbg_gT"], gT[:], deps=[gtprev])
    P.barrier()
    gst.close()
    xst.close()
    gros.close()

    b.sb(f"{tag}pad1", [128, 32], F32, st)
    AbsD = b.sb(f"{tag}AbsD", [128, 2 * T], FP16, st)
    b.sb(f"{tag}pad2", [128, 32], F32, st)
    it = P.gp(lambda e: e.iota(AbsD[:], pattern=[[1, 2 * T]], base=-2047,
                               channel_multiplier=-1,
                               allow_small_or_imprecise_dtypes=True))
    P.sc(lambda e: e.activation(AbsD[:], AbsD[:], AF.Abs), deps=[it])
    P.barrier()
    if "dbg_Acol" in dr:
        P.ydma(dr["dbg_Acol"], Acol[:])
        P.ydma(dr["dbg_Gcol"], Gcol[:])
        P.ydma(dr["dbg_PBcol"], PBcol[:])
        P.ydma(dr["dbg_AbsD"], AbsD[:])
        P.barrier()

    W = T // 2
    pen = b.sb(f"{tag}pen", [128, 16, W], BF16, st)
    sc1 = b.sb(f"{tag}sc1", [128, 3 * W], FP16, st)
    Sw = b.sb(f"{tag}Sw", [128, W], F32, st)
    Ebuf = [b.sb(f"{tag}E{i}", [128, W], BF16, st) for i in range(2)]
    Pbuf = [b.sb(f"{tag}P{i}", [128, W], BF16, st) for i in range(2)]
    spair = b.sb(f"{tag}spair", [128, W], F32, st)
    r1row = b.sb(f"{tag}r1row", [1, W], F32, st)
    RbA = b.sb(f"{tag}RbA", [64, W], F32, st)
    RbB = b.sb(f"{tag}RbB", [64, W], F32, st)
    tmA = b.sb(f"{tag}tmA", [64, W], FP16, st)
    tmB = b.sb(f"{tag}tmB", [64, W], FP16, st)
    addb = b.sb(f"{tag}addb", [128, W], FP16, st)
    xr = b.sb(f"{tag}xr", [128, W], F32, st)
    nxb = b.sb(f"{tag}nx", [128, W], F32, st)
    ps_sA = b.ps(f"{tag}psSA", [128, W], F32, st)
    ps_sB = b.ps(f"{tag}psSB", [128, W], F32, st)
    ps_cA = b.ps(f"{tag}psCA", [65, W], F32, st)
    ps_cB = b.ps(f"{tag}psCB", [65, W], F32, st)

    for qh in range(2):
        q0 = qh * W
        pen_done = []
        for cc in range(16):
            u = gT[:, cc:cc + 1]
            S1 = sc1[:, 0:W]
            S2 = sc1[:, W:2 * W]
            S3 = sc1[:, 2 * W:3 * W]
            off = q0 + 2047 - 128 * cc
            P.ve(lambda e, u2=u, q2=q0: e.tensor_scalar_sub(S1, Acol[:, q2:q2 + W], u2))
            P.ve(lambda e: e.tensor_scalar_max(S2, S1, 0.0))
            P.ve(lambda e, q2=q0: e.tensor_mul(S1, S2, Acol[:, q2:q2 + W]))
            P.ve(lambda e, u2=u, q2=q0: e.tensor_scalar_sub(S2, Gcol[:, q2:q2 + W], u2))
            P.ve(lambda e: e.tensor_scalar_max(S3, S2, 0.0))
            P.ve(lambda e, q2=q0: e.tensor_mul(S2, S3, Gcol[:, q2:q2 + W]))
            P.ve(lambda e, q2=q0: e.scalar_tensor_tensor(
                S3, S2, -1.0, Gcol[:, q2:q2 + W], ALU.mult, ALU.add))
            P.ve(lambda e: e.tensor_add(S2, S3, S1))
            P.ve(lambda e, o=off, q2=q0: e.scalar_tensor_tensor(
                Sw[:], AbsD[:, o:o + W], -1.0, PBcol[:, q2:q2 + W], ALU.mult, ALU.add))
            P.ve(lambda e: e.tensor_scalar(S3, Sw[:], 1.0, 0.0, ALU.min, ALU.max))
            pen_done.append(P.ve(lambda e, c2=cc: e.tensor_mul(pen[:, c2, :], S2, S3)))
        if "dbg_xn" in dr:
            dmp = P.ydma(dr["dbg_pen"][:, :, qh * W:(qh + 1) * W], pen[:],
                         deps=[pen_done[-1]])
            pen_done = pen_done + [dmp]
        tail_prev = []
        for i in range(NCH):
            exps = {}
            mults = {}
            ctx_mms = {}
            for cc in range(16):
                depA = [exps.get((cc - 1, 0)), exps.get((cc - 1, 1))] + \
                    (tail_prev if cc == 0 else [])
                mA = P.te(lambda e, i2=i, c2=cc, q2=q0: e.matmul(
                    ps_sA[:, 0:512], K[0:64, i2, 128 * c2:128 * (c2 + 1)],
                    Q[0:64, i2, q2:q2 + 512], start=True, stop=True), deps=depA)
                mB = P.te(lambda e, i2=i, c2=cc, q2=q0: e.matmul(
                    ps_sB[:, 0:512], K[64:128, i2, 128 * c2:128 * (c2 + 1)],
                    Q[64:128, i2, q2:q2 + 512], start=True, stop=True,
                    tile_position=(64, 0)), deps=depA)
                mA2 = P.te(lambda e, i2=i, c2=cc, q2=q0: e.matmul(
                    ps_sA[:, 512:W], K[0:64, i2, 128 * c2:128 * (c2 + 1)],
                    Q[0:64, i2, q2 + 512:q2 + W], start=True, stop=True))
                mB2 = P.te(lambda e, i2=i, c2=cc, q2=q0: e.matmul(
                    ps_sB[:, 512:W], K[64:128, i2, 128 * c2:128 * (c2 + 1)],
                    Q[64:128, i2, q2 + 512:q2 + W], start=True, stop=True,
                    tile_position=(64, 0)))
                eA = P.sc(lambda e: e.activation(Ebuf[0][:], ps_sA[:], AF.Exp),
                          deps=[mA, mA2, mults.get((cc - 1, 0))])
                eB = P.sc(lambda e: e.activation(Ebuf[1][:], ps_sB[:], AF.Exp),
                          deps=[mB, mB2, mults.get((cc - 1, 1))])
                exps[(cc, 0)], exps[(cc, 1)] = eA, eB
                pA = P.ve(lambda e, c2=cc: e.tensor_mul(Pbuf[0][:], Ebuf[0][:], pen[:, c2, :]),
                          deps=[eA, pen_done[cc], ctx_mms.get((cc - 1, 0))])
                pB = P.ve(lambda e, c2=cc: e.tensor_mul(Pbuf[1][:], Ebuf[1][:], pen[:, c2, :]),
                          deps=[eB, ctx_mms.get((cc - 1, 1))])
                mults[(cc, 0)], mults[(cc, 1)] = pA, pB
                lA = lB = None
                for nt in range(2):
                    lA = P.te(lambda e, i2=i, c2=cc, n=nt: e.matmul(
                        ps_cA[:, 512 * n:512 * (n + 1)],
                        Vaug[:, c2, 65 * (2 * i2):65 * (2 * i2) + 65],
                        Pbuf[0][:, 512 * n:512 * (n + 1)],
                        start=(c2 == 0), stop=(c2 == 15)),
                        deps=[pA] + (tail_prev if cc == 0 and nt == 0 else []))
                    lB = P.te(lambda e, i2=i, c2=cc, n=nt: e.matmul(
                        ps_cB[:, 512 * n:512 * (n + 1)],
                        Vaug[:, c2, 65 * (2 * i2 + 1):65 * (2 * i2 + 1) + 65],
                        Pbuf[1][:, 512 * n:512 * (n + 1)],
                        start=(c2 == 0), stop=(c2 == 15)),
                        deps=[pB])
                ctx_mms[(cc, 0)], ctx_mms[(cc, 1)] = lA, lB
            csA = P.ve(lambda e: e.tensor_copy(spair[64:65, 0:W], ps_cA[64:65, :]),
                       deps=[lA])
            rr1 = P.sc(lambda e: e.activation(spair[64:65, :], spair[64:65, :], AF.Ln),
                       deps=[csA])
            rr2 = P.sc(lambda e: e.activation(spair[64:65, :], spair[64:65, :], AF.Exp,
                                              scale=-1.0))
            dA = P.ydma(r1row[:, :], spair[64:65, 0:W], deps=[rr2])
            csB = P.ve(lambda e: e.tensor_copy(spair[64:65, 0:W], ps_cB[64:65, :]),
                       deps=[lB, dA])
            rr3 = P.sc(lambda e: e.activation(spair[64:65, :], spair[64:65, :], AF.Ln),
                       deps=[csB])
            rr4 = P.sc(lambda e: e.activation(spair[64:65, :], spair[64:65, :], AF.Exp,
                                              scale=-1.0))
            mmA = P.te(lambda e: e.matmul(ps_sA[0:64, 0:512], c["ones_row64"][:],
                                          r1row[:, 0:512], start=True, stop=True),
                       deps=[dA, exps[(15, 0)], exps[(15, 1)]])
            mmA2 = P.te(lambda e: e.matmul(ps_sA[0:64, 512:W], c["ones_row64"][:],
                                           r1row[:, 512:W], start=True, stop=True))
            eRbA = P.ve(lambda e: e.tensor_copy(RbA[:], ps_sA[0:64, :]), deps=[mmA, mmA2])
            dB = P.ydma(r1row[:, :], spair[64:65, 0:W], deps=[eRbA, rr4])
            mmB = P.te(lambda e: e.matmul(ps_sB[0:64, 0:512], c["ones_row64"][:],
                                          r1row[:, 0:512], start=True, stop=True),
                       deps=[dB])
            mmB2 = P.te(lambda e: e.matmul(ps_sB[0:64, 512:W], c["ones_row64"][:],
                                           r1row[:, 512:W], start=True, stop=True))
            eRbB = P.ve(lambda e: e.tensor_copy(RbB[:], ps_sB[0:64, :]), deps=[mmB, mmB2])
            vA = P.ve(lambda e: e.tensor_mul(tmA[:], ps_cA[0:64, :], RbA[:]), deps=[eRbA])
            vB = P.ve(lambda e: e.tensor_mul(tmB[:], ps_cB[0:64, :], RbB[:]), deps=[eRbB])
            dsh = P.ydma(addb[64:128, :], tmB[:], deps=[vB])
            xl = P.ydma(xr[:], x_src[i, :, q0:q0 + W], deps=tail_prev)
            nx1 = P.ve(lambda e: e.tensor_add(nxb[0:64, :], tmA[:], xr[0:64, :]),
                       deps=[vA, xl])
            nx2 = P.ve(lambda e: e.tensor_add(nxb[64:128, :], addb[64:128, :],
                                              xr[64:128, :]), deps=[dsh, xl])
            stx = P.ydma(x_dst[i, :, q0:q0 + W], nxb[:], deps=[nx1, nx2])
            tail_prev = [stx, vA, vB, eRbA, eRbB]
        P.barrier()
    st.close()


DBG_STG = [99]


def emit_attn_b(b: B, x_src, x_dst, dr, tag):
    STG = DBG_STG[0]
    P, c = b.P, b.c
    st = ExitStack()
    LBC = 256
    NCK = T // LBC
    xn = b.sb(f"{tag}xn", [128, NCH, T], BF16, st)
    for t0 in (0, T // 2):
        emit_layernorm(b, x_src, t0, T // 2, xn[:, :, t0:t0 + T // 2], tag + f"l{t0 // 1024}")
    P.barrier()

    xf = b.sb(f"{tag}xf", [128, NCH, T], F32, st)
    Wqb = b.sb(f"{tag}Wq", [128, NCH, 512], BF16, st)
    Wkb = b.sb(f"{tag}Wk", [128, NCH, 512], BF16, st)
    Wvb = b.sb(f"{tag}Wv", [128, NCH, 512], BF16, st)
    Wob = b.sb(f"{tag}Wo", [128, 4, 1024], BF16, st)
    bet = b.sb(f"{tag}bet", [1, 512], F32, st)
    Gb = b.sb(f"{tag}Gb", [128, 512], F32, st)
    G1 = b.sb(f"{tag}G1", [128, 512], F32, st)
    lds = [P.gdma(Wqb[:], dr["bWq"]), P.gdma(Wkb[:], dr["bWk"]),
           P.gdma(Wvb[:], dr["bWv"]), P.gdma(Wob[:], dr["bWo"]),
           P.ydma(bet[:], dr["bbet"])]
    for ch in range(NCH):
        lds.append(P.ydma(xf[:, ch, :], x_src[ch]))
    sgb = P.sc(lambda e: e.activation(bet[:], bet[:], AF.Sigmoid), deps=[lds[4]])

    mem_sb = b.sb(f"{tag}mem", [128, 512], BF16, st)
    mem32 = b.sb(f"{tag}mem32", [128, 512], F32, st)
    z_acc = b.sb(f"{tag}z", [128, 4], F32, st)
    ztmp = b.sb(f"{tag}zt", [128, 4], F32, st)
    zb = b.sb(f"{tag}zb", [128, 4], BF16, st)
    v_aug = b.sb(f"{tag}vaug", [128, 2, 8, 65], BF16, st)
    v_pl = b.sb(f"{tag}vpl", [128, 2, 512], BF16, st)
    qf = b.sb(f"{tag}qf", [128, 4, 256], BF16, st)
    kf = b.sb(f"{tag}kf", [128, 4, 256], BF16, st)
    sqf = b.sb(f"{tag}sqf", [128, 4, 256], BF16, st)
    skf = b.sb(f"{tag}skf", [128, 4, 256], BF16, st)
    qfp = b.sb(f"{tag}qfp", [128, 2, 4, 256], BF16, st)
    kfp = b.sb(f"{tag}kfp", [128, 2, 4, 256], BF16, st)
    sqp = b.sb(f"{tag}sqp", [128, 2, 4, 256], BF16, st)
    sk_tm = b.sb(f"{tag}sktm", [128, 2, 512], BF16, st)
    ET = b.sb(f"{tag}ET", [128, 8, 2, 256], BF16, st)
    elu_t = b.sb(f"{tag}elt", [128, 1024], FP16, st)
    elu_e = b.sb(f"{tag}ele", [128, 1024], BF16, st)
    sqz = b.sb(f"{tag}sqz", [128, 16], F32, st)
    rz = b.sb(f"{tag}rz", [128, 16], F32, st)
    rdc = b.sb(f"{tag}rdc", [128, 16], F32, st)
    at1 = b.sb(f"{tag}at1", [128, 64], FP16, st)
    at2 = b.sb(f"{tag}at2", [128, 64], FP16, st)
    at3 = b.sb(f"{tag}at3", [128, 64], FP16, st)
    att_tm = b.sb(f"{tag}attm", [128, 2, 512], F32, st)
    att_fm = b.sb(f"{tag}atfm", [128, 4, 256], BF16, st)
    nxb = b.sb(f"{tag}nxb", [128, 2, 256], F32, st)
    vgap = b.sb(f"{tag}vgap", [128, 1], F32, st)

    ps_mem = b.ps(f"{tag}psmem", [128, 512], F32, st)
    PS1 = b.ps(f"{tag}PS1", [128, 1024], F32, st)
    ps_sf = b.ps(f"{tag}pssf", [128, 512], F32, st)
    ps_st = b.ps(f"{tag}psst", [128, 512], F32, st)
    ps_ad = b.ps(f"{tag}psad", [128, 272], F32, st)
    ps_tp = b.ps(f"{tag}pstp", [128, 512], F32, st)
    ps_s1 = b.ps(f"{tag}pss1", [128, 164], F32, st)

    gmm = P.te(lambda e: e.matmul(PS1[:, 0:512], c["ones_row_f32"][:], bet[:],
                                  start=True, stop=True), deps=[sgb])
    egb = P.ve(lambda e: e.tensor_copy(Gb[:], PS1[:, 0:512]), deps=[gmm])
    vinit = P.ve(lambda e: e.memset(v_aug[:], 1.0))
    P.ve(lambda e: e.tensor_scalar(G1[:], Gb[:], -1.0, 1.0, ALU.mult, ALU.add))
    zinit = P.ve(lambda e: e.memset(z_acc[:], 1.0 / 64.0))
    minit = P.ve(lambda e: e.memset(mem_sb[:], 0.0))
    minit2 = P.ve(lambda e: e.memset(mem32[:], 0.0))
    for t_ in (qfp, kfp, sqp):
        P.ve(lambda e, T_=t_: e.memset(T_[64:128, 0], 0.0))
        P.ve(lambda e, T_=t_: e.memset(T_[0:64, 1], 0.0))
    P.barrier()

    mem_mm_last = None
    for ci in range(NCK):
        ti = slice(LBC * ci, LBC * (ci + 1))
        if ci > 0 and STG >= 2.6:
            mev = P.ve(lambda e: e.tensor_copy(mem_sb[:], mem32[:]))
        else:
            mev = None

        evq = evk = None
        for (nm, W_, ld_, tgt) in (("q", Wqb, lds[0], qf), ("k", Wkb, lds[1], kf)):
            last = None
            for oc in range(4):
                for kc in range(NCH):
                    last = P.te(lambda e, W2=W_, o=oc, k=kc, t2=ti,
                                s=(kc == 0), q=(kc == NCH - 1):
                                e.matmul(PS1[:, 256 * o:256 * (o + 1)],
                                         W2[:, k, 128 * o:128 * (o + 1)],
                                         xn[:, k, t2], start=s, stop=q),
                                deps=[ld_, evq] if (oc == 0 and kc == 0) else [])
            ev = P.sc(lambda e, T_=tgt: e.activation(
                T_[:].rearrange("p a b -> p (a b)"), PS1[:], AF.Copy), deps=[last])
            if nm == "q":
                evq = ev
            else:
                evk = ev
        vevs = []
        for tc in range(2):
            last = None
            for kc in range(NCH):
                last = P.te(lambda e, t=tc, k=kc, c2=ci,
                            s=(kc == 0), q=(kc == NCH - 1):
                            e.matmul(PS1[:, 512 * t:512 * (t + 1)],
                                     xn[:, k, 256 * c2 + 128 * t:256 * c2 + 128 * (t + 1)],
                                     Wvb[:, k, :], start=s, stop=q),
                            deps=[lds[2], evk] if kc == 0 else [])
            e1 = P.ve(lambda e, t=tc: e.tensor_copy(v_pl[:, t, :],
                                                    PS1[:, 512 * t:512 * (t + 1)]),
                      deps=[last, vinit])
            for hh in range(8):
                e1 = P.ve(lambda e, t=tc, h2=hh: e.tensor_copy(
                    v_aug[:, t, h2, 0:64], PS1[:, 512 * t + 64 * h2:512 * t + 64 * (h2 + 1)]),
                    deps=[])
            vevs.append(e1)
        elus = []
        for (src_, dst_, ev) in ((qf, sqf, evq), (kf, skf, evk)):
            m1 = P.ve(lambda e, S_=src_: e.tensor_scalar_min(
                elu_t[:], S_[:].rearrange("p a b -> p (a b)"), 0.0), deps=[ev])
            x1 = P.sc(lambda e: e.activation(elu_e[:], elu_t[:], AF.Exp), deps=[m1])
            s1 = P.ve(lambda e, S_=src_, D_=dst_: e.scalar_tensor_tensor(
                D_[:].rearrange("p a b -> p (a b)"),
                S_[:].rearrange("p a b -> p (a b)"), 0.0, elu_e[:],
                ALU.max, ALU.add), deps=[x1])
            elus.append(s1)
        pads = []
        for (src_, dst_, ev) in ((qf, qfp, evq), (kf, kfp, evk), (sqf, sqp, elus[0])):
            P.ve(lambda e, S_=src_, D_=dst_: e.tensor_copy(
                D_[0:64, 0], S_[0:64]), deps=[ev])
            pads.append(P.ve(lambda e, S_=src_, D_=dst_: e.tensor_copy(
                D_[64:128, 1], S_[64:128])))
        if "dbg_b_qf" in dr:
            P.ydma(dr["dbg_b_qf"][ci], qf[:], deps=[evq])
            P.ydma(dr["dbg_b_kf"][ci], kf[:], deps=[evk])
            P.ydma(dr["dbg_b_vpl"][ci], v_pl[:].rearrange("p a b -> p (a b)"),
                   deps=[vevs[0], vevs[1]])
            P.ydma(dr["dbg_b_mem"][ci], mem_sb[:], deps=[mev] if mev else [])
        if STG < 2:
            P.barrier()
            continue
        if STG < 2.2:
            P.barrier()
            continue
        ktev = []
        for tc in range(2):
            last = None
            for kc in range(NCH):
                last = P.te(lambda e, t=tc, k=kc, c2=ci,
                            s=(kc == 0), q=(kc == NCH - 1):
                            e.matmul(ps_tp[:, 0:512] if False else PS1[:, 512 * t:512 * (t + 1)],
                                     xn[:, k, 256 * c2 + 128 * t:256 * c2 + 128 * (t + 1)],
                                     Wkb[:, k, :], start=s, stop=q),
                            deps=[lds[1], vevs[1], elus[1]] if kc == 0 else [])
            ktev.append(P.ve(lambda e, t=tc: e.tensor_copy(
                sk_tm[:, t, :], PS1[:, 512 * t:512 * (t + 1)]), deps=[last]))
        me1 = P.ve(lambda e: e.tensor_scalar_min(
            elu_t[:], sk_tm[:, :, :].rearrange("p a b -> p (a b)"), 0.0),
            deps=[ktev[0], ktev[1]])
        me2 = P.sc(lambda e: e.activation(elu_e[:], elu_t[:], AF.Exp), deps=[me1])
        me3 = P.ve(lambda e: e.scalar_tensor_tensor(
            sk_tm[:, :, :].rearrange("p a b -> p (a b)"),
            sk_tm[:, :, :].rearrange("p a b -> p (a b)"), 0.0, elu_e[:],
            ALU.max, ALU.add), deps=[me2])
        sktev = [me3, me3]
        if "dbg_b_sktm" in dr:
            P.ydma(dr["dbg_b_sktm"][ci], sk_tm[:].rearrange("p a b -> p (a b)"),
                   deps=[me3])
        if STG < 2.4:
            P.barrier()
            continue
        zred = None
        for c2 in range(4):
            zred = P.ve(lambda e, cc=c2: e.reduce_sum(ztmp[:, cc:cc + 1],
                                                      skf[:, cc, :], axis=AX.X),
                        deps=[elus[1]] if c2 == 0 else [])
        zbc = P.ve(lambda e: e.tensor_copy(zb[:], z_acc[:]))
        zupd = P.ve(lambda e: e.tensor_add(z_acc[:], z_acc[:], ztmp[:]),
                    deps=[zred, zbc])
        if "dbg_b_zb" in dr:
            P.ydma(dr["dbg_b_zb"][ci], zb[:], deps=[zbc])
        if STG < 2.6:
            P.barrier()
            continue
        for tc in range(2):
            for c2 in range(4):
                mem_mm_last = P.te(lambda e, t=tc, cc=c2,
                                   s=(tc == 0 and c2 == 0),
                                   q=(tc == 1 and c2 == 3):
                                   e.matmul(ps_mem[:, 128 * cc:128 * (cc + 1)],
                                            sk_tm[:, t, 128 * cc:128 * (cc + 1)],
                                            v_pl[:, t, 128 * cc:128 * (cc + 1)],
                                            start=s, stop=q),
                                   deps=[sktev[tc], vevs[tc]] +
                                   ([mev] if (tc == 0 and c2 == 0 and mev) else []))
        P.ve(lambda e: e.scalar_tensor_tensor(mem32[:], ps_mem[:], 1.0, mem32[:],
                                              ALU.mult, ALU.add), deps=[mem_mm_last])
        if STG < 3:
            P.barrier()
            continue
        echain = pads[1]
        for h in range(8):
            for pc in range(2):
                mm1 = P.te(lambda e, h2=h, p2=pc: e.matmul(
                    ps_st[:, 256 * p2:256 * (p2 + 1)],
                    kfp[:, h2 % 2, h2 // 2, 128 * p2:128 * (p2 + 1)],
                    qf[:, h2 // 2, :],
                    start=True, stop=True), deps=[echain])
                echain = P.sc(lambda e, h2=h, p2=pc: e.activation(
                    ET[:, h2, p2, :], ps_st[:, 256 * p2:256 * (p2 + 1)], AF.Exp,
                    scale=0.125), deps=[mm1])
        if "dbg_b_ET" in dr:
            P.ydma(dr["dbg_b_ET"][ci], ET[:].rearrange("p a b c -> p (a b c)"),
                   deps=[echain])
        if STG < 6:
            P.barrier()
            continue
        am_mms = []
        for h in range(8):
            for lc in range(2):
                w = 2 * h + lc
                am_mms.append(P.te(lambda e, h2=h, l2=lc, w2=w: e.matmul(
                    PS1[:, 64 * w2:64 * (w2 + 1)],
                    sqp[:, h2 % 2, h2 // 2, 128 * l2:128 * (l2 + 1)],
                    mem_sb[:, 128 * (h2 // 2) + 64 * (h2 % 2):
                           128 * (h2 // 2) + 64 * (h2 % 2) + 64],
                    start=True, stop=True),
                    deps=[pads[2], mev, vevs[1]] if (h == 0 and lc == 0) else []))
        if STG < 6.3:
            P.barrier()
            continue
        for h in range(8):
            for lc in range(2):
                w = 2 * h + lc
                am_mms.append(P.te(lambda e, h2=h, l2=lc, w2=w: e.matmul(
                    ps_s1[:, 128 + 2 * w2:128 + 2 * w2 + 1],
                    sqp[:, h2 % 2, h2 // 2, 128 * l2:128 * (l2 + 1)],
                    zb[:, h2 // 2:h2 // 2 + 1],
                    start=True, stop=True),
                    deps=[zbc] if (h == 0 and lc == 0) else []))
        if STG < 6.6:
            P.barrier()
            continue
        esz = P.ve(lambda e: e.tensor_copy(
            sqz[:], ps_s1[:, 128:160].rearrange("p (a b) -> p a b", b=2)[:, :, 0]),
            deps=[am_mms[-1]])
        P.ve(lambda e: e.memset(vgap[:], 0.0))
        erz = P.ve(lambda e: e.reciprocal(rz[:], sqz[:]))
        if "dbg_b_rz" in dr:
            P.ydma(dr["dbg_b_rz"][ci], rz[:], deps=[erz])
        if STG < 7:
            P.barrier()
            continue
        ad_chain = None
        cchain = erz
        for h in range(8):
            for lc in range(2):
                w = 2 * h + lc
                aw = w % 4
                last = None
                for pc in range(2):
                    last = P.te(lambda e, h2=h, l2=lc, p2=pc, a2=aw,
                                s=(pc == 0), q=(pc == 1):
                                e.matmul(ps_ad[:, 68 * a2:68 * a2 + 65],
                                         ET[:, h2, p2, 128 * l2:128 * (l2 + 1)],
                                         v_aug[:, p2, h2, :], start=s, stop=q),
                                deps=[echain, ad_chain] if pc == 0 else [])
                cp = P.ve(lambda e, a2=aw, w2=w: e.tensor_copy(
                    rdc[:, w2:w2 + 1], ps_ad[:, 68 * a2 + 64:68 * a2 + 65]), deps=[last])
                t1 = P.ve(lambda e, w2=w: e.tensor_scalar_mul(
                    at1[:], PS1[:, 64 * w2:64 * (w2 + 1)], rz[:, w2:w2 + 1]),
                    deps=[am_mms[-1], erz])
                rc2 = P.ve(lambda e, w2=w: e.reciprocal(rdc[:, w2:w2 + 1],
                                                        rdc[:, w2:w2 + 1]))
                g1_ = P.ve(lambda e, h2=h: e.tensor_mul(at3[:], at1[:], Gb[:, 64 * h2:64 * (h2 + 1)]),
                           deps=[egb])
                t2 = P.ve(lambda e, a2=aw, w2=w: e.tensor_scalar_mul(
                    at2[:], ps_ad[:, 68 * a2:68 * a2 + 64], rdc[:, w2:w2 + 1]),
                    deps=[rc2])
                P.ve(lambda e: e.memset(vgap[:], 0.0))
                g2_ = P.ve(lambda e, h2=h: e.tensor_mul(at1[:], at2[:], G1[:, 64 * h2:64 * (h2 + 1)]))
                P.ve(lambda e: e.memset(vgap[:], 0.0))
                cchain = P.ve(lambda e, l2=lc, h2=h: e.tensor_add(
                    att_tm[:, l2, 64 * h2:64 * (h2 + 1)], at3[:], at1[:]))
                ad_chain = t2
        if "dbg_b_rdc" in dr:
            P.ydma(dr["dbg_b_rdc"][ci], rdc[:], deps=[cchain])
            P.ydma(dr["dbg_b_attm"][ci], att_tm[:].rearrange("p a b -> p (a b)"),
                   deps=[cchain])
        if STG < 8:
            P.barrier()
            continue
        tchain2 = cchain
        for lc in range(2):
            tts = []
            for c2 in range(4):
                tt_ = P.te(lambda e, l2=lc, cc=c2: e.transpose(
                    ps_tp[:, 128 * cc:128 * (cc + 1)],
                    att_tm[:, l2, 128 * cc:128 * (cc + 1)], c["ident_f32"][:]),
                    deps=[tchain2, sktev[1]])
                tts.append(tt_)
            for c2 in range(4):
                tchain2 = P.ve(lambda e, l2=lc, cc=c2: e.tensor_copy(
                    att_fm[:, cc, 128 * l2:128 * (l2 + 1)],
                    ps_tp[:, 128 * cc:128 * (cc + 1)]),
                    deps=[tts[3]] if c2 == 0 else [])
        if "dbg_b_atfm" in dr:
            P.ydma(dr["dbg_b_atfm"][ci], att_fm[:], deps=[tchain2])
        if STG < 9:
            P.barrier()
            continue
        wchain = tchain2
        wstores = []
        for dc in range(NCH):
            pt = ps_sf if dc % 2 == 0 else ps_st
            last = None
            for kc in range(4):
                last = P.te(lambda e, d2=dc, k=kc, p=pt, s=(kc == 0), q=(kc == 3):
                            e.matmul(p[:, 0:256], Wob[:, k, 128 * d2:128 * (d2 + 1)],
                                     att_fm[:, k, :], start=s, stop=q),
                            deps=[wchain, lds[3]] if kc == 0 else [])
            nx = P.ve(lambda e, d2=dc, p=pt, t2=ti: e.tensor_add(
                nxb[:, d2 % 2, :], p[:, 0:256], xf[:, d2, t2]),
                deps=[last, lds[5 + dc]] + ([wstores[dc - 2]] if dc >= 2 else []))
            wstores.append(P.ydma(x_dst[dc, :, ti], nxb[:, dc % 2, :], deps=[nx]))
            if dc % 2 == 1:
                wchain = nx
        P.barrier()
    st.close()


def _tile_w(w, n_oc, n_kc):
    O, I = w.shape
    assert O == n_oc * 128 and I == n_kc * 128
    return np.ascontiguousarray(
        w.reshape(n_oc, 128, n_kc, 128).transpose(0, 3, 2, 1))


def _scales_pp(s, n_oc):
    return np.ascontiguousarray(s.reshape(n_oc, 128).T)


def stage_inputs(inputs):
    g = {}
    g["w1t"] = _tile_w(inputs["L0_ffn_w1_w"], 32, 8)
    g["w2t"] = _tile_w(inputs["L0_ffn_w2_w"], 32, 8)
    g["h1t"] = _tile_w(inputs["L0_ffn_h1_w"], 32, 32)
    g["h2t"] = _tile_w(inputs["L0_ffn_h2_w"], 32, 32)
    g["w3t"] = _tile_w(inputs["L0_ffn_w3_w"], 8, 32)
    g["w1s"] = _scales_pp(inputs["L0_ffn_w1_s"], 32)
    g["w2s"] = _scales_pp(inputs["L0_ffn_w2_s"], 32)
    g["h1s"] = _scales_pp(inputs["L0_ffn_h1_s"], 32)
    g["h2s"] = _scales_pp(inputs["L0_ffn_h2_s"], 32)
    g["w3s"] = _scales_pp(inputs["L0_ffn_w3_s"], 8)
    for nm, key in (("aWq", "L0_attn_a_Wq"), ("aWk", "L0_attn_a_Wk"), ("aWv", "L0_attn_a_Wv")):
        g[nm] = np.ascontiguousarray(
            inputs[key].reshape(NCH, 128, D).transpose(1, 0, 2))
    g["aWg"] = np.ascontiguousarray(inputs["L0_attn_a_Wg"].reshape(NCH, 128).T)
    g["abq"] = np.ascontiguousarray(inputs["L0_attn_a_bq"].reshape(NCH, 128).T)
    g["abk"] = np.ascontiguousarray(inputs["L0_attn_a_bk"].reshape(NCH, 128).T)
    g["abg"] = np.ascontiguousarray(np.broadcast_to(
        inputs["L0_attn_a_bg"].reshape(1, 1), (128, 1)))
    g["abv"] = np.ascontiguousarray(inputs["L0_attn_a_bv"].reshape(1, D))
    g["ident"] = np.eye(128, dtype=np.float32)
    for nm, key in (("bWq", "L0_attn_b_Wq"), ("bWk", "L0_attn_b_Wk"), ("bWv", "L0_attn_b_Wv")):
        g[nm] = np.ascontiguousarray(
            inputs[key].reshape(NCH, 128, 512).transpose(1, 0, 2))
    g["bWo"] = np.ascontiguousarray(
        inputs["L0_attn_b_Wo"].reshape(4, 128, 1024).transpose(1, 0, 2))
    g["bbet"] = np.ascontiguousarray(inputs["L0_attn_b_betas"].reshape(1, 512))
    return g


def build_program(sublayers, debug=False):
    nc = bass.Bass("TRN2", target_bir_lowering=False, detect_race_conditions=False)
    dr = {}
    dr["x0"] = nc.dram_tensor("x0", [NCH, 128, T], F32, kind="ExternalInput").ap()
    for nm, sh in [("w1t", [32, 128, 8, 128]), ("w2t", [32, 128, 8, 128]),
                   ("h1t", [32, 128, 32, 128]), ("h2t", [32, 128, 32, 128]),
                   ("w3t", [8, 128, 32, 128])]:
        dr[nm] = nc.dram_tensor(nm, sh, F32, kind="ExternalInput").ap()
    for nm in ["w1s", "w2s", "h1s", "h2s", "w3s"]:
        n_oc = 8 if nm == "w3s" else 32
        dr[nm] = nc.dram_tensor(nm, [128, n_oc], F32, kind="ExternalInput").ap()
    dr["aWq"] = nc.dram_tensor("aWq", [128, NCH, D], F32, kind="ExternalInput").ap()
    dr["aWk"] = nc.dram_tensor("aWk", [128, NCH, D], F32, kind="ExternalInput").ap()
    dr["aWv"] = nc.dram_tensor("aWv", [128, NCH, D], F32, kind="ExternalInput").ap()
    dr["aWg"] = nc.dram_tensor("aWg", [128, NCH], F32, kind="ExternalInput").ap()
    dr["abq"] = nc.dram_tensor("abq", [128, NCH], F32, kind="ExternalInput").ap()
    dr["abk"] = nc.dram_tensor("abk", [128, NCH], F32, kind="ExternalInput").ap()
    dr["abg"] = nc.dram_tensor("abg", [128, 1], F32, kind="ExternalInput").ap()
    dr["abv"] = nc.dram_tensor("abv", [1, D], F32, kind="ExternalInput").ap()
    dr["ident"] = nc.dram_tensor("ident", [128, 128], F32, kind="ExternalInput").ap()
    dr["bWq"] = nc.dram_tensor("bWq", [128, NCH, 512], F32, kind="ExternalInput").ap()
    dr["bWk"] = nc.dram_tensor("bWk", [128, NCH, 512], F32, kind="ExternalInput").ap()
    dr["bWv"] = nc.dram_tensor("bWv", [128, NCH, 512], F32, kind="ExternalInput").ap()
    dr["bWo"] = nc.dram_tensor("bWo", [128, 4, 1024], F32, kind="ExternalInput").ap()
    dr["bbet"] = nc.dram_tensor("bbet", [1, 512], F32, kind="ExternalInput").ap()
    dr["out"] = nc.dram_tensor("out", [NCH, 128, T], F32, kind="ExternalOutput").ap()
    dr["xs"] = nc.dram_tensor("xs", [NCH, 128, T], F32, kind="Internal").ap()
    if debug:
        dr["dbg_xn"] = nc.dram_tensor("dbg_xn", [128, NCH, T], BF16, kind="ExternalOutput").ap()
        dr["dbg_Q"] = nc.dram_tensor("dbg_Q", [128, NCH, T], BF16, kind="ExternalOutput").ap()
        dr["dbg_K"] = nc.dram_tensor("dbg_K", [128, NCH, T], BF16, kind="ExternalOutput").ap()
        dr["dbg_V"] = nc.dram_tensor("dbg_V", [128, 16, 1040], BF16, kind="ExternalOutput").ap()
        dr["dbg_g"] = nc.dram_tensor("dbg_g", [1, T], F32, kind="ExternalOutput").ap()
        dr["dbg_gT"] = nc.dram_tensor("dbg_gT", [128, 16], F32, kind="ExternalOutput").ap()
        dr["dbg_pen"] = nc.dram_tensor("dbg_pen", [128, 16, T], BF16, kind="ExternalOutput").ap()
        dr["dbg_raw"] = nc.dram_tensor("dbg_raw", [1, T], F32, kind="ExternalOutput").ap()
        dr["dbg_Acol"] = nc.dram_tensor("dbg_Acol", [128, T], FP16, kind="ExternalOutput").ap()
        dr["dbg_Gcol"] = nc.dram_tensor("dbg_Gcol", [128, T], FP16, kind="ExternalOutput").ap()
        dr["dbg_PBcol"] = nc.dram_tensor("dbg_PBcol", [128, T], F32, kind="ExternalOutput").ap()
        dr["dbg_AbsD"] = nc.dram_tensor("dbg_AbsD", [128, 2 * T], FP16, kind="ExternalOutput").ap()
        dr["dbg_rb"] = nc.dram_tensor("dbg_rb", [1, 2 * T], F32, kind="ExternalOutput").ap()
        dr["dbg_swg"] = nc.dram_tensor("dbg_swg", [1, NCH + 1], F32, kind="ExternalOutput").ap()
        for nm, shp, dt in (
                ("dbg_b_qf", [8, 128, 4, 256], BF16), ("dbg_b_kf", [8, 128, 4, 256], BF16),
                ("dbg_b_sktm", [8, 128, 1024], BF16), ("dbg_b_vpl", [8, 128, 1024], BF16),
                ("dbg_b_mem", [8, 128, 512], BF16), ("dbg_b_zb", [8, 128, 4], BF16),
                ("dbg_b_ET", [8, 128, 8, 512], BF16),
                ("dbg_b_rz", [8, 128, 16], F32), ("dbg_b_rdc", [8, 128, 16], F32),
                ("dbg_b_attm", [8, 128, 1024], F32), ("dbg_b_atfm", [8, 128, 4, 256], BF16)):
            dr[nm] = nc.dram_tensor(nm, shp, dt, kind="ExternalOutput").ap()

    with ExitStack() as ctx:
        b = B(nc, ctx)
        P = b.P
        block = ctx.enter_context(nc.Block())
        emit_prologue(b)

        idl = P.ydma(b.c["ident_f32"][:], dr["ident"])
        P.ve(lambda e: e.tensor_copy(b.c["ident_bf"][:], b.c["ident_f32"][:]), deps=[idl])
        scl = {}
        for nm, n_oc in [("w1s", 32), ("w2s", 32), ("h1s", 32), ("h2s", 32), ("w3s", 8)]:
            t = b.sb("sb_" + nm, [128, n_oc], F32)
            P.ydma(t[:], dr[nm])
            scl[nm[:-1] + "_s"] = t
        P.barrier()

        wts = {"w1": dr["w1t"], "w2": dr["w2t"], "h1": dr["h1t"], "h2": dr["h2t"],
               "w3": dr["w3t"], **scl}

        cur = dr["x0"]
        for li, sl in enumerate(sublayers):
            dst = dr["out"] if li == len(sublayers) - 1 else dr["xs"]
            if sl == "ffn":
                for t0 in (0, T // 2):
                    emit_ffn_tile(b, cur, dst, t0, T // 2, wts, f"f{li}t{t0 // 1024}")
                    P.barrier()
            elif sl == "attn_a":
                emit_attn_a(b, cur, dst, dr, f"a{li}_")
                P.barrier()
            elif sl == "attn_b":
                emit_attn_b(b, cur, dst, dr, f"b{li}_")
                P.barrier()
            else:
                raise NotImplementedError(sl)
            cur = dst

        P.barrier()
        P.emit(block, ctx)
    return nc


def kernel(**inputs):
    debug = inputs.pop("_debug", False)
    sublayers = inputs.pop("_sublayers", None) or ["attn_a", "ffn", "attn_b", "ffn"]
    x = np.asarray(inputs["L0_attn_a_hidden_states"])
    g = stage_inputs(inputs)
    nc = build_program(sublayers, debug=debug)
    in_maps = []
    for core in range(8):
        m = dict(g)
        m["x0"] = np.ascontiguousarray(x[core].T.reshape(NCH, 128, T))
        in_maps.append(m)
    res = run_bass_kernel_spmd(nc, in_maps, core_ids=list(range(8)))
    outs = []
    for core in range(8):
        o = res.results[core]["out"]
        outs.append(o.reshape(D, T).T)
    ret = np.stack(outs).astype(np.float32)
    if debug:
        kernel.dbg = res.results
    return ret
